# version 05
# baseline (speedup 1.0000x reference)
import dataclasses
import numpy as np
import ml_dtypes

import concourse.bass as bass
import concourse.mybir as mybir
from concourse.bass_utils import run_bass_kernel_spmd

F32 = mybir.dt.float32
BF16 = mybir.dt.bfloat16
AF = mybir.ActivationFunctionType
ALU = mybir.AluOpType
BF = ml_dtypes.bfloat16

D = 2048
TOK = 1024
SEQ = 4096
H = 8
DV = 256
KD = 16
KF = 64
NEG = -3.0e38
RT128 = float(1.0 / np.sqrt(128.0))
GROUPS = [[0, 1, 2, 3], [4, 5, 6, 7]]
ENGS = ("tensor", "vector", "scalar", "gpsimd", "sync")


class Sem:
    __slots__ = ("h", "n")

    def __init__(self, nc, name):
        self.h = nc.alloc_semaphore(name=name)
        self.n = 0


class KB:
    def __init__(self, nc):
        self.nc = nc
        self.streams = {e: [] for e in ENGS}

    def sem(self, name):
        return Sem(self.nc, name)

    def wait(self, eng, sem, val=None):
        v = sem.n if val is None else val
        if v > 0:
            self.streams[eng].append(lambda e, h=sem.h, v=v: e.wait_ge(h, v))

    def raw_wait1(self, eng, handle):
        self.streams[eng].append(lambda e, h=handle: e.wait_ge(h, 1))

    def op(self, eng, build, inc=None, inc_n=1):
        if inc is not None:
            inc.n += inc_n

        def f(e, build=build, inc=inc, inc_n=inc_n):
            ins = build(e)
            if inc is not None:
                ins.then_inc(inc.h, inc_n)
        self.streams[eng].append(f)
        return inc.n if inc is not None else None

    def chained(self, eng, build, sem):
        prev = sem.n
        sem.n += 1

        def f(e, build=build, h=sem.h, prev=prev):
            if prev > 0:
                e.wait_ge(h, prev)
            build(e).then_inc(h, 1)
        self.streams[eng].append(f)
        return sem.n


def bc_row(dram_ap, row, col0, n, p=128):
    a = dram_ap[row:row + 1, col0:col0 + n]
    return dataclasses.replace(a, ap=[[0, p], [1, n]])


def bc_elem(dram_ap, row, col, p=128):
    a = dram_ap[row:row + 1, col:col + 1]
    return dataclasses.replace(a, ap=[[0, p], [1, 1]])


def tile_lhsT(w):
    Kf, Mf = w.shape
    ka, mb = Kf // 128, Mf // 128
    return np.ascontiguousarray(
        w.reshape(ka, 128, mb, 128).transpose(2, 1, 0, 3)
        .reshape(mb, 128, ka * 128).astype(np.float32))


def build_nc(taps=False):
    nc = bass.Bass("TRN2", target_bir_lowering=False, debug=False)
    kb = KB(nc)
    dram = {}

    def din(name, shape, dt=F32):
        dram[name] = nc.dram_tensor(name, list(shape), dt, kind="ExternalInput").ap()

    def dout(name, shape, dt=F32):
        dram[name] = nc.dram_tensor(name, list(shape), dt, kind="ExternalOutput").ap()

    def dint(name, shape, dt=F32):
        dram[name] = nc.dram_tensor(name, list(shape), dt, kind="Internal").ap()

    din("x_pre", [128, KD * TOK])
    din("wq_t", [8, 128, KD * 128])
    din("wk_t", [8, 128, KD * 128])
    din("wv_t", [KD, 128, KD * 128])
    din("wif_b", [128, KD * 16])
    din("wog_t", [KD, 128, KD * 128])
    din("wout_t", [KD, 128, KD * 128])
    din("wga_t", [KF, 128, KD * 128])
    din("wua_t", [KF, 128, KD * 128])
    din("wda_t", [KD, 128, KF * 128])
    din("wgf_t", [KF, 128, KD * 128])
    din("wuf_t", [KF, 128, KD * 128])
    din("wdf_t", [KD, 128, KF * 128])
    din("n1col", [128, KD])
    din("n2col", [128, KD])
    din("mhcol", [128, KD])
    din("bif", [16, 1])
    din("gamma", [128, 1])
    din("tri", [128, 128])
    din("identb", [128, 128], BF16)
    din("identf", [32, 32])
    din("onesb", [128, 1], BF16)
    din("addmask", [128, 8])
    din("onehotF", [8, 4])
    din("initmask", [8, 1])

    dout("xout", [128, KD * TOK])
    if taps:
        dout("tap_hin", [128, KD * TOK], BF16)
        dout("tap_hout", [128, KD * TOK], BF16)
        dout("tap_s1", [128, KD * TOK])
        dout("tap_s2", [128, KD * TOK])
        dout("tap_s3", [128, KD * TOK])
        dout("tap_vec", [8, 4 * TOK])
        dout("tap_bd", [8, SEQ])
        dout("tap_zsb", [128, 2112], BF16)
        dout("tap_bcols", [128, 192])
        dout("tap_biasA", [128, 64])
        dout("tap_ered", [512, D])
        for _sc in range(2):
            dout(f"tap_ctt{_sc}", [128, 2048], BF16)
            dout(f"tap_htmp{_sc}", [128, 1024])
            dout(f"tap_sden{_sc}", [1, 512])
            dout(f"tap_nrow{_sc}", [1, 512])
        dout("tap_zb", [128, 2112], BF16)
        dout("tap_w4", [128, 8])
        dout("tap_kwB", [128, 1024], BF16)
        dout("tap_bcd", [1, 24])
        dout("tap_at", [D, 512], BF16)
        dout("tap_recip", [4, 512])

    dint("cc_kv_in", [TOK, 3072], BF16)
    dint("cc_kv_out", [4 * TOK, 3072], BF16)
    dint("cc_if_in", [16, TOK])
    dint("cc_if_out", [64, TOK])
    dint("q_d", [8, 128, TOK], BF16)
    dint("sigo_d", [KD, 128, TOK], BF16)
    dint("b_d", [8, SEQ])
    dint("ownb_d", [8, TOK])
    dint("ownM_d", [8, TOK])
    dint("ownen_d", [8, TOK])
    dint("bc_d", [1, 32])
    dint("bcx_d", [1, 32])
    dint("sm_d", [1, 384])
    dint("rows_d", [16, 3 * 512])
    dint("lnrows_d", [24, 1024])
    dint("E_part_d", [D, D])
    dint("E_red_d", [512, D])
    dint("attnT_in_d", [D, 512], BF16)
    dint("attnT_out_d", [4 * D, 512], BF16)
    dint("recip_in_d", [1, 512])
    dint("recip_out_d", [4, 512])

    def S_(name, shape, dt=F32):
        return nc.alloc_sbuf_tensor(name, list(shape), dt)

    def carve(name, shape, dt, base, off):
        return nc.alloc_sbuf_tensor_at(name, list(shape), dt, offset=base + off)

    xres = S_("xres", [128, KD * TOK])
    bigA = S_("bigA", [128, KD * TOK], BF16)
    bigB = S_("bigB", [128, KD * TOK], BF16)
    arenaM = S_("arenaM", [128, 24576], mybir.dt.uint8)
    wb = [S_(f"wb{i}", [128, KD * 128], BF16) for i in range(6)]
    stg32 = S_("stg32", [128, 1024])
    smallbf = S_("smallbf", [128, 1664], BF16)
    sbB = nc.lookup_mloc(smallbf).addr
    silu_sb = carve("silu_sb", [128, 512], BF16, sbB, 0)
    sigt = carve("sigt", [128, 1024], BF16, sbB, 1024)
    stage = carve("stage", [128, 1024], BF16, sbB, 0)
    qstr = carve("qstr", [128, 512], BF16, sbB, 0)
    negmax = carve("negmax", [128, 1], F32, sbB, 3072)
    esum_a = carve("esum_a", [128, 1], F32, sbB, 3104)
    esum_b = carve("esum_b", [128, 1], F32, sbB, 3136)
    erec4 = carve("erec4", [128, 4], F32, sbB, 3168)
    rgcol = carve("rgcol", [128, 1], F32, sbB, 3200)
    RA = S_("RA", [1, 1024])
    RB = S_("RB", [1, 1024])
    RC = S_("RC", [1, 1024])
    sb_s2 = S_("sb_s2", [128, 1024])
    sb_m2 = S_("sb_m2", [128, 1024])
    n1col = S_("n1c", [128, KD])
    n2col = S_("n2c", [128, KD])
    mhcol = S_("mhc", [128, KD])
    bifsb = S_("bifsb", [16, 1])
    bif15 = S_("bif15", [16, 1])
    gcol = S_("gcol", [128, 1])
    tri = S_("tri_s", [128, 128])
    identb = S_("identb_s", [128, 128], BF16)
    identf = S_("identf_s", [32, 32])
    onesb = S_("onesb_s", [128, 1], BF16)
    addmask = S_("addmask_s", [128, 8])
    bc_sb = S_("bc_sb", [1, 40])


    rowA = RA[0:1, 0:1024]
    rowB = RB[0:1, 0:1024]
    rowC = RC[0:1, 0:1024]
    rowD = stg32[0:1, 0:1024]
    scale_r = RA[0:1, 0:512]
    den_r = RA[0:1, 512:1024]
    den_r2 = RB[0:1, 0:512]
    enrow = RB[0:1, 512:1024]
    mrow_s = RC[0:1, 0:512]

    amB = nc.lookup_mloc(arenaM).addr
    ktb = carve("ktb", [128, 4096], BF16, amB, 0)
    vtb = carve("vtb", [128, 8192], BF16, amB, 8192)
    VE = carve("VE", [128, 6144], F32, amB, 0)
    iown = VE[0:8, 0:1024]
    fown = VE[0:8, 1024:2048]
    olog = VE[0:8, 2048:3072]
    ofcum = VE[0:8, 3072:4096]
    t8a = VE[0:8, 4096:4100]
    t8b = VE[0:8, 4100:4104]
    onehotF = VE[0:8, 4104:4108]
    ownb = VE[32:40, 0:1024]
    ownM = VE[32:40, 1024:2048]
    ownm2 = VE[32:40, 2048:3072]
    ofcum32 = VE[32:40, 3072:4096]
    bc1 = VE[32:40, 4096:4097]
    bc2 = VE[32:40, 4097:4098]
    fb8 = VE[32:40, 4098:4099]
    expfac = VE[32:40, 4099:4100]
    scini = VE[32:40, 4100:4101]
    initmask = VE[32:40, 4101:4102]
    fb8b = VE[0:8, 4108:4109]
    ifp = VE[64:80, 0:1024]
    t32a = stg32[0:32, 256:384]

    baB = nc.lookup_mloc(bigA).addr
    VA = carve("VA", [128, 4096], F32, baB, 0)
    VB = carve("VB", [128, 4096], F32, baB, 16384)
    ig = VA[0:8, 0:4096]
    fg = VA[32:40, 0:4096]
    lsg = VA[64:72, 0:4096]
    fcum = VB[0:8, 0:4096]
    bglob = VB[32:40, 0:4096]
    mglob = VB[64:72, 0:4096]
    scb = carve("scb", [128, 512], F32, baB, 0)
    rcb = carve("rcb", [128, 512], F32, baB, 2048)
    rfb = carve("rfb", [128, 512], F32, baB, 4096)
    mbc = carve("mbc", [128, 512], F32, baB, 6144)
    smat = carve("smat", [128, 512], F32, baB, 8192)
    htmp = carve("htmp", [128, 1024], F32, baB, 10240)
    ctt = carve("ctt", [128, 2048], BF16, baB, 14336)
    hsq = carve("hsq", [128, 1024], BF16, baB, 18432)
    kw = carve("kw", [128, 1024], BF16, baB, 20480)
    ko1 = carve("ko1", [128, 512], BF16, baB, 22528)
    zsb = carve("zsb", [128, 2112], BF16, baB, 23552)
    bcols = carve("bcols", [128, 192], F32, baB, 27776)
    obcols = carve("obcols", [128, 64], F32, baB, 28800)
    biasA = carve("biasA", [128, 64], F32, baB, 29056)
    biasB = carve("biasB", [128, 8], F32, baB, 29312)
    bccol = carve("bccol", [128, 16], F32, baB, 29344)
    w4 = carve("w4", [128, 8], F32, baB, 29408)

    sb_b = carve("sb_b", [128, 1024], F32, nc.lookup_mloc(wb[4]).addr, 0)
    sb_m = carve("sb_m", [128, 1024], F32, nc.lookup_mloc(wb[5]).addr, 0)

    nc.const_aps.register(F32, 1e-5, bc_sb[0:1, 32:33])
    nc.const_aps.register(F32, 1e-6, bc_sb[0:1, 33:34])

    ps = [nc.alloc_psum_tensor(f"ps{i}", [128, 512], F32) for i in range(7)]
    psT = nc.alloc_psum_tensor("psT", [128, 1024], BF16)

    sy = kb.sem("sy")
    gd = kb.sem("gd")
    ccs = [kb.sem(f"cc{i}") for i in range(5)]
    tm = kb.sem("tm")
    vm = kb.sem("vm")
    sm = kb.sem("sm")

    def dma_s(out, in_, slow=False):
        return kb.op("sync", lambda e, o=out, i=in_, slow=slow: e.dma_start(
            out=o, in_=i, allow_slow_non_contiguous=slow), inc=sy, inc_n=16)

    def dma_g(out, in_):
        return kb.op("gpsimd", lambda e, o=out, i=in_: e.dma_start(out=o, in_=i),
                     inc=gd, inc_n=16)

    def ten(build, inc=None):
        return kb.op("tensor", build, inc=inc)

    def vec(build, inc=None):
        return kb.chained("vector", build, vm)

    def sca(build, inc=None):
        return kb.chained("scalar", build, sm)

    fbuf = S_("fbuf", [1, 16])

    def vfence(n=1):
        for _ in range(n):
            vec(lambda e: e.memset(fbuf[0:1, 0:8], 0.0))
        return vm.n

    def sfence(n=1):
        for _ in range(n):
            sca(lambda e: e.activation(fbuf[0:1, 8:9], bc_sb[0:1, 32:33],
                                       AF.Copy))
        return sm.n

    def coll(kind, op, ins, outs, sem):
        kb.streams["gpsimd"].append(
            lambda e, kind=kind, op=op, ins=ins, outs=outs, h=sem.h:
            e.collective_compute(kind, op, replica_groups=GROUPS, ins=ins,
                                 outs=outs).then_inc(h))

    wlast = {}

    dma_s(xres[:, :], dram["x_pre"])
    for nm, dst in [("n1col", n1col[:, :]), ("n2col", n2col[:, :]),
                    ("mhcol", mhcol[:, :]), ("bif", bifsb[:, :]),
                    ("gamma", gcol[:, :]), ("tri", tri[:, :]),
                    ("identb", identb[:, :]), ("identf", identf[:, :]),
                    ("onesb", onesb[:, :]), ("addmask", addmask[:, :]),
                    ("onehotF", onehotF), ("initmask", initmask)]:
        dma_s(dst, dram[nm])
    consts_loaded = sy.n

    kb.wait("scalar", sy, consts_loaded)
    sca(lambda e: e.mul(bif15[:, :], bifsb[:, :], 1.0 / 15.0), inc=sm)
    bif15_ready = sm.n

    def cast_to_bf16(dst, src):
        for a in range(KD):
            vec(lambda e, a=a: e.tensor_copy(dst[:, a * TOK:(a + 1) * TOK],
                                             src[:, a * TOK:(a + 1) * TOK]),
                inc=(vm if a == KD - 1 else None))
        return vm.n

    kb.wait("vector", sy, consts_loaded)
    vec(lambda e: e.memset(bc_sb[0:1, 32:33], 1e-5))
    vec(lambda e: e.memset(fbuf[0:1, 0:8], 0.0))
    vec(lambda e: e.memset(bc_sb[0:1, 33:34], 1e-6), inc=vm)
    kb.wait("scalar", vm)
    sfence()
    cast_to_bf16(bigA, xres)

    def ln_stats(src_bf):
        kb.wait("tensor", vm)
        kb.wait("tensor", sm)
        kb.wait("scalar", vm)
        prev_ms = tm.n
        for a in range(KD):
            kb.wait("scalar", tm, prev_ms)
            sq_ms = sca(lambda e, a=a: e.activation(
                stage[:, 0:1024], src_bf[:, a * TOK:(a + 1) * TOK],
                AF.Square), inc=sm)
            kb.wait("tensor", sm, sq_ms)
            for half in range(2):
                ten(lambda e, a=a, half=half: e.matmul(
                    ps[half][0:1, 0:512], onesb[:, 0:1],
                    src_bf[:, a * TOK + half * 512: a * TOK + half * 512 + 512],
                    start=(a == 0), stop=(a == KD - 1)))
                prev_ms = ten(lambda e, a=a, half=half: e.matmul(
                    ps[2 + half][0:1, 0:512], onesb[:, 0:1],
                    stage[:, half * 512: half * 512 + 512],
                    start=(a == 0), stop=(a == KD - 1)),
                    inc=(tm if half == 1 else None)) or prev_ms
        return tm.n

    def ln_rows(want_comb):
        kb.wait("vector", tm)
        for half in range(2):
            hs = slice(half * 512, half * 512 + 512)
            vec(lambda e, half=half, hs=hs: e.tensor_scalar(
                rowA[:, hs], ps[half][0:1, 0:512], 1.0 / D, None, ALU.mult))
            vec(lambda e, half=half, hs=hs: e.tensor_scalar(
                rowB[:, hs], ps[2 + half][0:1, 0:512], 1.0 / D, None, ALU.mult))
        vec(lambda e: e.tensor_mul(rowC[:, :], rowA[:, :], rowA[:, :]))
        vec(lambda e: e.tensor_sub(rowB[:, :], rowB[:, :], rowC[:, :]), inc=vm)
        kb.wait("scalar", vm)
        sca(lambda e: e.activation(rowC[:, :], rowB[:, :], AF.Ln, bias=1e-5))
        sca(lambda e: e.activation(rowC[:, :], rowC[:, :], AF.Exp, scale=-0.5),
            inc=sm)
        kb.wait("vector", sm)
        if want_comb:
            vec(lambda e: e.tensor_mul(rowD[:, :], rowC[:, :], rowC[:, :]))
            vec(lambda e: e.tensor_mul(rowD[:, :], rowD[:, :], rowB[:, :]), inc=vm)
            kb.wait("scalar", vm)
            sca(lambda e: e.activation(rowD[:, :], rowD[:, :], AF.Ln, bias=1e-6))
            sca(lambda e: e.activation(rowD[:, :], rowD[:, :], AF.Exp, scale=-0.5),
                inc=sm)
            kb.wait("vector", sm)
            vec(lambda e: e.tensor_mul(rowB[:, :], rowC[:, :], rowD[:, :]))
            vec(lambda e: e.tensor_mul(rowD[:, :], rowA[:, :], rowB[:, :]))
            vec(lambda e: e.tensor_scalar(rowD[:, :], rowD[:, :], -1.0, None,
                                          ALU.mult))
        vec(lambda e: e.tensor_mul(rowA[:, :], rowA[:, :], rowC[:, :]))
        vec(lambda e: e.tensor_scalar(rowA[:, :], rowA[:, :], -1.0, None,
                                      ALU.mult), inc=vm)
        return vm.n

    def row_bcasts(phase, rows_targets):
        lnr = dram["lnrows_d"]
        vfence()
        sfence()
        kb.wait("sync", vm)
        kb.wait("sync", sm)
        for i, (r, _) in enumerate(rows_targets):
            dma_s(lnr[phase * 4 + i: phase * 4 + i + 1, :], r)
        wrote = sy.n
        kb.wait("sync", sy, wrote)
        for i, (_, t) in enumerate(rows_targets):
            for half in range(2):
                dma_s(t[:, half * 512:half * 512 + 512],
                      bc_row(lnr, phase * 4 + i, half * 512, 512))
        return sy.n

    ln_stats(bigA)
    ln_rows(True)
    row_bcasts(0, [(rowB, sb_b), (rowD, sb_m), (rowC, sb_s2), (rowA, sb_m2)])
    bcasts1 = sy.n

    kb.wait("vector", sy, bcasts1)
    for a in range(KD):
        for half in range(2):
            c0 = a * TOK + half * 512
            b0 = half * 512
            vec(lambda e, a=a, c0=c0, b0=b0: e.scalar_tensor_tensor(
                stg32[:, 0:512], xres[:, c0:c0 + 512], n1col[:, a:a + 1],
                sb_b[:, b0:b0 + 512], ALU.mult, ALU.mult))
            vec(lambda e, a=a, c0=c0, b0=b0: e.scalar_tensor_tensor(
                bigB[:, c0:c0 + 512], sb_m[:, b0:b0 + 512], n1col[:, a:a + 1],
                stg32[:, 0:512], ALU.mult, ALU.add),
                inc=(vm if (a == KD - 1 and half == 1) else None))
    hin_ready = vm.n
    if taps:
        kb.wait("sync", vm, hin_ready)
        dma_s(dram["tap_hin"], bigB[:, :])

    def proj(w_nm, n_mtiles, bufs, evac_cb):
        for mi in range(n_mtiles):
            bi = bufs[mi % 2]
            kb.wait("gpsimd", tm, wlast.get(f"b{bi}", 0))
            ld = dma_g(wb[bi][:, :], dram[w_nm][mi])
            kb.wait("tensor", gd, ld)
            pa = (mi % 2) * 2
            pb = pa + 1
            kb.wait("tensor", vm)
            kb.wait("tensor", sm)
            for half in range(2):
                pp = pa if half == 0 else pb
                for a in range(KD):
                    ten(lambda e, half=half, a=a, bi=bi, pp=pp: e.matmul(
                        ps[pp][:, 0:512], wb[bi][:, a * 128:(a + 1) * 128],
                        bigB[:, a * TOK + half * 512: a * TOK + half * 512 + 512],
                        start=(a == 0), stop=(a == KD - 1)),
                        inc=(tm if (a == KD - 1 and half == 1) else None))
            wlast[f"b{bi}"] = tm.n
            evac_cb(mi, pa, pb, tm.n)

    kb.wait("tensor", vm, hin_ready)

    def tokmajor_evac(dst_col0):
        def cb(mi, pa, pb, ms):
            kb.wait("vector", tm, ms)
            kb.wait("vector", sy)
            vec(lambda e, pa=pa: e.tensor_copy(stage[:, 0:512], ps[pa][:, 0:512]))
            vec(lambda e, pb=pb: e.tensor_copy(stage[:, 512:1024],
                                               ps[pb][:, 0:512]), inc=vm)
            ev = vm.n
            kb.wait("tensor", vm, ev)
            for tt in range(8):
                ten(lambda e, tt=tt: e.transpose(
                    psT[:, tt * 128:(tt + 1) * 128],
                    stage[:, tt * 128:(tt + 1) * 128], identb[:, :]),
                    inc=(tm if tt == 7 else None))
            trd = tm.n
            kb.wait("vector", tm, trd)
            kb.wait("vector", sy)
            vec(lambda e: e.tensor_copy(ktb[:, 0:1024], psT[:, 0:1024]))
            ev2 = vfence()
            kb.wait("sync", vm, ev2)
            dma_s(dram["cc_kv_in"][0:1024, dst_col0 + mi * 128:
                                   dst_col0 + (mi + 1) * 128]
                  .rearrange("(tt p) d -> p tt d", p=128),
                  ktb[:, 0:1024].rearrange("p (tt d) -> p tt d", d=128))
        return cb

    proj("wk_t", 8, (0, 1), tokmajor_evac(0))
    proj("wv_t", KD, (2, 3), tokmajor_evac(1024))

    kb.wait("gpsimd", vm, hin_ready)
    wif_l = dma_g(wb[4][:, 0:256], dram["wif_b"][:, :])
    kb.wait("tensor", gd, wif_l)
    for half in range(2):
        kb.wait("tensor", sm)
        for a in range(KD):
            ten(lambda e, half=half, a=a: e.matmul(
                ps[4][0:16, 0:512], wb[4][:, a * 16:(a + 1) * 16],
                bigB[:, a * TOK + half * 512: a * TOK + half * 512 + 512],
                start=(a == 0), stop=(a == KD - 1)),
                inc=(tm if a == KD - 1 else None))
        if_done = tm.n
        kb.wait("scalar", tm, if_done)
        kb.wait("scalar", sm, bif15_ready)
        sca(lambda e, half=half: e.activation(
            ifp[:, half * 512:half * 512 + 512], ps[4][0:16, 0:512],
            AF.Tanh, bias=bif15[0:16, 0:1], scale=1.0 / 15.0), inc=sm)
        kb.wait("vector", sm)
        vec(lambda e, half=half: e.tensor_scalar(
            ifp[:, half * 512:half * 512 + 512],
            ifp[:, half * 512:half * 512 + 512], 15.0, None, ALU.mult), inc=vm)
    vfence()
    ifp_ready = vm.n
    kb.wait("sync", vm, ifp_ready)
    dma_s(dram["cc_if_in"][:, :], ifp)
    dma_s(iown, ifp[0:8, :])
    dma_s(fown, ifp[8:16, :])
    kv_staged = sy.n

    kb.wait("gpsimd", sy, kv_staged)
    coll("AllGather", ALU.bypass, [dram["cc_if_in"]], [dram["cc_if_out"]], ccs[0])
    coll("AllGather", ALU.bypass, [dram["cc_kv_in"]], [dram["cc_kv_out"]], ccs[1])

    def q_evac(mi, pa, pb, ms):
        kb.wait("vector", tm, ms)
        kb.wait("vector", sy)
        vec(lambda e, pa=pa: e.tensor_copy(stage[:, 0:512], ps[pa][:, 0:512]))
        vec(lambda e, pb=pb: e.tensor_copy(stage[:, 512:1024], ps[pb][:, 0:512]))
        vfence()
        kb.wait("sync", vm)
        dma_s(dram["q_d"][mi], stage[:, 0:1024])

    proj("wq_t", 8, (0, 1), q_evac)

    def og_evac(mi, pa, pb, ms):
        kb.wait("scalar", tm, ms)
        kb.wait("scalar", sy)
        sca(lambda e, pa=pa: e.activation(stage[:, 0:512], ps[pa][:, 0:512],
                                          AF.Sigmoid))
        sca(lambda e, pb=pb: e.activation(stage[:, 512:1024], ps[pb][:, 0:512],
                                          AF.Sigmoid))
        sfence()
        kb.wait("sync", sm)
        dma_s(dram["sigo_d"][mi], stage[:, 0:1024])

    proj("wog_t", KD, (2, 3), og_evac)
    sigo_ready = sy.n

    kb.raw_wait1("sync", ccs[0].h)
    for r in range(4):
        dma_s(ig[:, r * TOK:(r + 1) * TOK], dram["cc_if_out"][r * 16:r * 16 + 8, :])
        dma_s(fg[:, r * TOK:(r + 1) * TOK],
              dram["cc_if_out"][r * 16 + 8:r * 16 + 16, :])
    if_gathered = sy.n

    def logsig(dst, src, n, scr):
        for s0 in range(0, n, 1024):
            w = min(1024, n - s0)
            kb.wait("scalar", vm)
            sca(lambda e, s0=s0, w=w: e.activation(
                scr[0:8, 0:w], src[0:8, s0:s0 + w], AF.Abs))
            sca(lambda e, w=w: e.activation(
                scr[0:8, 0:w], scr[0:8, 0:w], AF.Exp, scale=-1.0))
            sca(lambda e, w=w: e.activation(
                scr[0:8, 0:w], scr[0:8, 0:w], AF.Ln, bias=1.0), inc=sm)
            kb.wait("vector", sm)
            vec(lambda e, s0=s0, w=w: e.tensor_scalar(
                dst[0:8, s0:s0 + w], src[0:8, s0:s0 + w], 0.0, None, ALU.min))
            vec(lambda e, s0=s0, w=w: e.tensor_sub(
                dst[0:8, s0:s0 + w], dst[0:8, s0:s0 + w], scr[0:8, 0:w]),
                inc=vm)

    kb.wait("scalar", sy, if_gathered)
    kb.wait("vector", sy, if_gathered)
    logsig(lsg, fg, SEQ, stg32[64:72, 0:1024])
    vec(lambda e: e.tensor_tensor_scan(fcum[:, :], lsg[:, :], lsg[:, :], 0.0,
                                       ALU.add, ALU.bypass), inc=vm)
    kb.wait("vector", vm)
    vec(lambda e: e.tensor_sub(bglob[:, :], ig[:, :], fcum[:, :]))
    vec(lambda e: e.tensor_tensor_scan(mglob[:, :], bglob[:, :], bglob[:, :], NEG,
                                       ALU.max, ALU.bypass), inc=vm)
    kb.wait("vector", vm)
    vec(lambda e: e.tensor_copy(t8a, fcum[:, 1023:4096:1024]))
    vec(lambda e: e.tensor_mul(t8a, t8a, onehotF))
    vec(lambda e: e.tensor_add(t8a[:, 0:2], t8a[:, 0:2], t8a[:, 2:4]))
    vec(lambda e: e.tensor_add(fb8, t8a[:, 0:1], t8a[:, 1:2]), inc=vm)
    vec(lambda e: e.tensor_copy(t8b, mglob[:, 1023:4096:1024]))
    vec(lambda e: e.tensor_mul(t8b, t8b, onehotF))
    vec(lambda e: e.tensor_add(t8b[:, 0:2], t8b[:, 0:2], t8b[:, 2:4]))
    vec(lambda e: e.tensor_add(bc1, t8b[:, 0:1], t8b[:, 1:2]), inc=vm)
    vfence(4)
    kb.wait("sync", vm)
    dma_s(dram["bcx_d"][0:1, 0:8].rearrange("r c -> c r"), fb8)
    dma_s(dram["bcx_d"][0:1, 8:16].rearrange("r c -> c r"), bc1)
    bcx_out = sy.n
    kb.wait("sync", sy, bcx_out)
    dma_s(fb8, dram["bcx_d"][0:1, 0:8].rearrange("r c -> c r"))
    dma_s(bc1, dram["bcx_d"][0:1, 8:16].rearrange("r c -> c r"))
    bcx_in = sy.n
    kb.wait("vector", sy, bcx_in)
    vec(lambda e: e.tensor_add(scini, bc1, initmask), inc=vm)

    logsig(olog, fown, TOK, stg32[0:8, 0:1024])
    vec(lambda e: e.tensor_tensor_scan(ofcum, olog, olog, 0.0, ALU.add,
                                       ALU.bypass), inc=vm)
    kb.wait("vector", vm)
    vec(lambda e: e.tensor_sub(ownb, iown, ofcum), inc=vm)
    kb.wait("vector", vm)
    vec(lambda e: e.tensor_scalar(ownb, ownb, fb8[:, 0:1], None, ALU.subtract),
        inc=vm)
    kb.wait("vector", vm)
    vec(lambda e: e.tensor_tensor_scan(ownM, ownb, ownb, scini[:, 0:1], ALU.max,
                                       ALU.bypass))
    vfence()
    kb.wait("sync", vm)
    dma_s(dram["ownM_d"][:, :], ownM)
    ownM_out = sy.n
    kb.wait("sync", sy, ownM_out)
    dma_s(bc2, dram["ownM_d"][:, 511:512], slow=True)
    kb.wait("vector", sy)
    vec(lambda e: e.tensor_copy(ofcum32, ofcum), inc=vm)
    kb.wait("vector", vm)
    vec(lambda e: e.tensor_add(ownm2, ofcum32, ownM), inc=vm)
    kb.wait("vector", vm)
    vec(lambda e: e.tensor_scalar(ownm2, ownm2, fb8[:, 0:1], None, ALU.add))
    vec(lambda e: e.tensor_sub(expfac, bc1, bc2), inc=vm)
    own_vec_done = vm.n
    kb.wait("scalar", vm, own_vec_done)
    sca(lambda e: e.activation(ownm2, ownm2, AF.Exp, scale=-1.0))
    sca(lambda e: e.activation(expfac, expfac, AF.Exp), inc=sm)
    own_sca_done = sm.n

    vfence(4)
    sfence(4)
    kb.wait("sync", sm)
    kb.wait("sync", vm)
    dma_s(dram["b_d"][:, :], bglob)
    dma_s(dram["ownb_d"][:, :], ownb)
    dma_s(dram["ownen_d"][:, :], ownm2)
    dma_s(dram["bc_d"][0:1, 0:8].rearrange("r c -> c r"), bc1)
    dma_s(dram["bc_d"][0:1, 8:16].rearrange("r c -> c r"), bc2)
    dma_s(dram["bc_d"][0:1, 16:24].rearrange("r c -> c r"), expfac)
    dma_s(dram["bc_d"][0:1, 24:32].rearrange("r c -> c r"), fb8)
    vecs_in_dram = sy.n
    if taps:
        kb.wait("sync", sy, vecs_in_dram)
        dma_s(dram["tap_bd"], dram["b_d"][:, :])
        dma_s(dram["tap_vec"][0:8, 0:1024], ownb)
        dma_s(dram["tap_vec"][0:8, 1024:2048], ownM)
        dma_s(dram["tap_vec"][0:8, 2048:3072], ownm2)
        dma_s(dram["tap_vec"][0:8, 3072:4096], ofcum)
    kb.wait("sync", sy, vecs_in_dram)
    dma_s(bc_sb[0:1, 0:32], dram["bc_d"][0:1, :])
    for h in range(H):
        dma_s(bccol[:, h:h + 1], bc_elem(dram["bc_d"], 0, h))
        dma_s(bccol[:, H + h:H + h + 1], bc_elem(dram["bc_d"], 0, 16 + h))
        dma_s(biasB[:, h:h + 1], bc_elem(dram["bc_d"], 0, 8 + h))
    bccol_ready = sy.n
    kb.wait("vector", sy, bccol_ready)
    vec(lambda e: e.tensor_scalar(biasB[:, :], biasB[:, :], -1.0, None, ALU.mult))
    for h in range(H):
        vec(lambda e, h=h: e.tensor_scalar(
            biasA[:, h * 8:(h + 1) * 8], addmask[:, :], bccol[:, h:h + 1], None,
            ALU.subtract))
    bias_ready = vfence()

    kb.wait("tensor", sy, vecs_in_dram)
    for src_nm, width, dstt, dw in (("b_d", 32, bcols, 24),
                                    ("ownb_d", 8, obcols, 8)):
        for h in range(H):
            kb.wait("sync", vm)
            kb.wait("sync", sm)
            rr = width
            ld = dma_s(t32a[0:rr, :], dram[src_nm][h:h + 1, :].rearrange(
                "r (a c) -> (r a) c", c=128))
            kb.wait("tensor", sy, ld)
            kb.wait("tensor", vm)
            ten(lambda e, rr=rr: e.transpose(
                ps[4][:, 0:rr], t32a[0:rr, :], identf[0:rr, 0:rr]), inc=tm)
            kb.wait("vector", tm)
            vec(lambda e, h=h, dstt=dstt, dw=dw, rr=rr: e.tensor_copy(
                dstt[:, h * dw:h * dw + min(rr, dw)], ps[4][:, 0:min(rr, dw)]),
                inc=vm)
    bcols_ready = vfence()
    if taps:
        kb.wait("sync", vm, bcols_ready)
        dma_s(dram["tap_bcols"], bcols[:, :])
        dma_s(dram["tap_biasA"], biasA[:, :])

    kb.raw_wait1("sync", ccs[1].h)

    def load_chunk(src_rows_ap):
        kb.wait("sync", tm)
        kb.wait("sync", vm)
        dma_s(ktb[:, :].rearrange("p (a c) -> p a c", c=1024),
              src_rows_ap[:, 0:1024].rearrange("(a p) c -> p a c", p=128))
        dma_s(vtb[:, :].rearrange("p (a c) -> p a c", c=2048),
              src_rows_ap[:, 1024:3072].rearrange("(a p) c -> p a c", p=128))
        return sy.n

    kb.wait("tensor", sy, sigo_ready)
    kb.wait("tensor", vm, bias_ready)
    kb.wait("tensor", vm, bcols_ready)

    def w4_stage(op):
        slot, bias_col, bcol4 = op["slot"], op["bias"], op["bcol4"]
        kb.wait("scalar", vm)
        sca(lambda e, slot=slot, bcol4=bcol4, bias_col=bias_col: e.activation(
            w4[:, slot * 4:slot * 4 + 4], bcol4, AF.Exp, bias=bias_col))

    def dz_mm(op):
        h, slot, pz = op["h"], op["slot"], op["pz"]
        first, last = op["first"], op["last"]
        kb.wait("vector", sm)
        kb.wait("vector", tm)
        for tt in range(4):
            vec(lambda e, h=h, tt=tt, slot=slot: e.tensor_scalar(
                kw[:, slot * 512 + tt * 128: slot * 512 + (tt + 1) * 128],
                ktb[:, tt * 1024 + h * 128: tt * 1024 + (h + 1) * 128],
                w4[:, slot * 4 + tt: slot * 4 + tt + 1], None, ALU.mult))
        kwv = vfence()
        kb.wait("tensor", vm, kwv)
        for tt in range(4):
            ten(lambda e, h=h, tt=tt, slot=slot, pz=pz, first=first, last=last:
                e.matmul(
                    pz[:, 0:256],
                    kw[:, slot * 512 + tt * 128: slot * 512 + (tt + 1) * 128],
                    vtb[:, tt * 2048 + h * 256: tt * 2048 + (h + 1) * 256],
                    start=(first and tt == 0), stop=(last and tt == 3)))
            ten(lambda e, h=h, tt=tt, slot=slot, pz=pz, first=first,
                last=last: e.matmul(
                    pz[:, 256:257],
                    kw[:, slot * 512 + tt * 128: slot * 512 + (tt + 1) * 128],
                    onesb[:, 0:1],
                    start=False, stop=(last and tt == 3)),
                inc=(tm if tt == 3 else None))
        return tm.n

    def run_dz_seq(seq):
        ms_map = {}
        w4_stage(seq[0])
        if len(seq) > 1:
            w4_stage(seq[1])
        sfence(4)
        for i, op in enumerate(seq):
            if op.get("pre") is not None:
                op["pre"]()
            ms_map[i] = dz_mm(op)
            if op.get("post") is not None:
                op["post"](op)
            if i + 2 < len(seq):
                w4_stage(seq[i + 2])
        return ms_map

    zend = {}
    for wave in range(2):
        heads = list(range(wave * 4, wave * 4 + 4))
        seq = []
        for hc in range(6):
            def pre(hc=hc):
                lc = load_chunk(dram["cc_kv_out"][hc * 512:(hc + 1) * 512, :])
                kb.wait("vector", sy, lc)
            for h in heads:
                seq.append(dict(h=h, slot=len(seq) % 2,
                                bias=biasA[:, h * 8 + hc:h * 8 + hc + 1],
                                bcol4=bcols[:, h * 24 + hc * 4:
                                            h * 24 + hc * 4 + 4],
                                pz=ps[h % 4], first=hc == 0, last=hc == 5,
                                pre=pre if h == heads[0] else None))
        ms_map = run_dz_seq(seq)
        for i, h in enumerate(heads):
            zend[h] = ms_map[20 + i]
        for h in heads:
            kb.wait("vector", tm, zend[h])
            vec(lambda e, h=h: e.tensor_copy(zsb[:, h * 264:h * 264 + 257],
                                             ps[h % 4][:, 0:257]), inc=vm)
        vfence()
        kb.wait("tensor", vm)
    zA_ready = vm.n
    if taps:
        kb.wait("sync", vm, zA_ready)
        dma_s(dram["tap_zsb"], zsb[:, :])

    def mixer_sc(sc):
        lc = load_chunk(dram["cc_kv_in"][sc * 512:(sc + 1) * 512, :])
        for h in range(H):
            hs = h * 2 + sc
            kb.wait("sync", tm)
            kb.wait("sync", vm)
            lq = dma_s(qstr[:, :], dram["q_d"][h][:, sc * 512:(sc + 1) * 512])
            dma_s(sigt[:, 0:512], dram["sigo_d"][2 * h][:, sc * 512:
                                                        (sc + 1) * 512])
            ls2 = dma_s(sigt[:, 512:1024],
                        dram["sigo_d"][2 * h + 1][:, sc * 512:(sc + 1) * 512])
            ld = dma_s(mrow_s, dram["ownM_d"][h:h + 1, sc * 512:(sc + 1) * 512])
            le = dma_s(enrow, dram["ownen_d"][h:h + 1, sc * 512:(sc + 1) * 512])
            lm = dma_s(mbc[:, :], bc_row(dram["ownM_d"], h, sc * 512, 512))
            kb.wait("vector", sy, lm)
            vec(lambda e, h=h, sc=sc: e.tensor_scalar(
                scale_r, mrow_s, bc_sb[0:1, sc * 8 + h:sc * 8 + h + 1],
                None, ALU.subtract), inc=vm)
            kb.wait("scalar", vm)
            sca(lambda e: e.activation(scale_r, scale_r, AF.Exp, scale=-1.0))
            sfence()
            kb.wait("vector", sm)
            vec(lambda e: e.tensor_scalar(scale_r, scale_r, RT128, None,
                                          ALU.mult))
            srow_done = vfence()
            kb.wait("sync", vm, srow_done)
            dma_s(dram["rows_d"][hs:hs + 1, 0:512], scale_r)
            kb.wait("sync", sy)
            lscb = dma_s(scb[:, :], bc_row(dram["rows_d"], hs, 0, 512))

            kb.wait("tensor", sy, lc)
            kb.wait("tensor", vm)
            for tt in range(4):
                ten(lambda e, h=h, tt=tt: e.transpose(
                    psT[:, tt * 128:(tt + 1) * 128],
                    ktb[:, tt * 1024 + h * 128: tt * 1024 + (h + 1) * 128],
                    identb[:, :]), inc=(tm if tt == 3 else None))
            trd = tm.n
            kb.wait("vector", tm, trd)
            vec(lambda e: e.tensor_copy(ko1[:, 0:512], psT[:, 0:512]))
            ko1_done = vfence()
            kb.wait("tensor", vm, ko1_done)
            kb.wait("tensor", sy, lq)
            for tt in range(4):
                n0 = tt * 128
                kb.wait("tensor", vm)
                ten(lambda e, tt=tt, n0=n0: e.matmul(
                    ps[4][:, n0:512], ko1[:, tt * 128:(tt + 1) * 128],
                    qstr[:, n0:512], start=True, stop=True), inc=tm)
                qkd = tm.n
                kb.wait("scalar", sy, lm)
                kb.wait("scalar", vm)
                kb.wait("vector", sy, lm)
                vec(lambda e, n0=n0: e.tensor_sub(
                    stg32[:, 0:128], tri[:, :], mbc[:, n0:n0 + 128]))
                dmsk = vfence()
                kb.wait("scalar", vm, dmsk)
                sca(lambda e, h=h, sc=sc, tt=tt, n0=n0: e.activation(
                    smat[:, n0:n0 + 128], stg32[:, 0:128], AF.Exp,
                    bias=obcols[:, h * 8 + sc * 4 + tt: h * 8 + sc * 4 + tt + 1]))
                sca(lambda e, h=h, sc=sc, tt=tt, n0=n0: e.activation(
                    smat[:, n0 + 128:512], mbc[:, n0 + 128:512], AF.Exp,
                    scale=-1.0,
                    bias=obcols[:, h * 8 + sc * 4 + tt: h * 8 + sc * 4 + tt + 1])
                    if n0 + 128 < 512 else
                    e.activation(smat[0:1, 0:1], mbc[0:1, 0:1], AF.Exp))
                smd = sfence()
                kb.wait("vector", tm, qkd)
                kb.wait("vector", sm, smd)
                vec(lambda e, tt=tt, n0=n0: e.scalar_tensor_tensor(
                    ctt[:, tt * 512 + n0: tt * 512 + 512], ps[4][:, n0:512],
                    RT128, smat[:, n0:512], ALU.mult, ALU.mult))
                cttd = vfence()
                kb.wait("tensor", vm, cttd)
                for dvt in range(2):
                    ten(lambda e, h=h, tt=tt, n0=n0, dvt=dvt: e.matmul(
                        ps[5 + dvt][:, n0:512],
                        vtb[:, tt * 2048 + h * 256 + dvt * 128:
                            tt * 2048 + h * 256 + (dvt + 1) * 128],
                        ctt[:, tt * 512 + n0: tt * 512 + 512],
                        start=(tt == 0), stop=(tt == 3)))
                ten(lambda e, tt=tt, n0=n0: e.matmul(
                    ps[0][0:1, n0:512], onesb[:, 0:1],
                    ctt[:, tt * 512 + n0: tt * 512 + 512],
                    start=(tt == 0), stop=(tt == 3)),
                    inc=(tm if tt == 3 else None))

            if taps and h == H - 1:
                kb.wait("sync", vm, cttd)
                dma_s(dram[f"tap_ctt{sc}"], ctt[:, :])
                kb.wait("vector", sy)
            kb.wait("tensor", vm)
            ten(lambda e, h=h: e.matmul(
                ps[1][:, 0:512], zsb[:, h * 264:h * 264 + 128], qstr[:, 0:512],
                start=True, stop=True))
            ten(lambda e, h=h: e.matmul(
                ps[2][:, 0:512], zsb[:, h * 264 + 128:h * 264 + 256],
                qstr[:, 0:512], start=True, stop=True))
            ten(lambda e, h=h: e.matmul(
                ps[3][0:1, 0:512], zsb[:, h * 264 + 256:h * 264 + 257],
                qstr[:, 0:512], start=True, stop=True), inc=tm)
            inter_done = tm.n

            kb.wait("vector", tm, inter_done)
            kb.wait("vector", sy, lscb)
            vec(lambda e: e.tensor_mul(den_r, ps[3][0:1, 0:512], scale_r))
            vec(lambda e: e.tensor_add(den_r2, den_r, ps[0][0:1, 0:512]))
            vfence()
            if taps and h == H - 1:
                kb.wait("sync", vm)
                dma_s(dram[f"tap_sden{sc}"], den_r2)
                kb.wait("scalar", sy)
            kb.wait("scalar", vm)
            sca(lambda e: e.activation(den_r2, den_r2, AF.Abs))
            sfence()
            kb.wait("vector", sm)
            kb.wait("vector", sy, le)
            vec(lambda e: e.tensor_max(den_r2, den_r2, enrow))
            vec(lambda e: e.reciprocal(den_r, den_r2))
            nrec = vfence()
            if taps and h == H - 1:
                kb.wait("sync", vm, nrec)
                dma_s(dram[f"tap_nrow{sc}"], den_r2)
                kb.wait("scalar", sy)
            kb.wait("sync", vm, nrec)
            dma_s(dram["rows_d"][hs:hs + 1, 512:1024], den_r)
            kb.wait("sync", sy)
            lrc = dma_s(rcb[:, :], bc_row(dram["rows_d"], hs, 512, 512))

            kb.wait("vector", sy, lrc)
            for dvt in range(2):
                vec(lambda e, dvt=dvt: e.tensor_mul(
                    stg32[:, dvt * 512:(dvt + 1) * 512], ps[1 + dvt][:, 0:512],
                    scb[:, :]))
                vec(lambda e, dvt=dvt: e.tensor_add(
                    htmp[:, dvt * 512:(dvt + 1) * 512],
                    stg32[:, dvt * 512:(dvt + 1) * 512], ps[5 + dvt][:, 0:512]))
                vec(lambda e, dvt=dvt: e.tensor_mul(
                    htmp[:, dvt * 512:(dvt + 1) * 512],
                    htmp[:, dvt * 512:(dvt + 1) * 512], rcb[:, :]))
            hdone = vfence()
            if taps and h == H - 1:
                kb.wait("sync", vm, hdone)
                dma_s(dram[f"tap_htmp{sc}"], htmp[:, :])
                kb.wait("vector", sy)
            kb.wait("scalar", vm, hdone)
            sca(lambda e: e.activation(hsq[:, 0:512], htmp[:, 0:512], AF.Square))
            sca(lambda e: e.activation(hsq[:, 512:1024], htmp[:, 512:1024],
                                       AF.Square))
            sqd = sfence()
            kb.wait("tensor", sm, sqd)
            kb.wait("tensor", vm)
            for dvt in range(2):
                ten(lambda e, dvt=dvt: e.matmul(
                    ps[3][0:1, 0:512], onesb[:, 0:1],
                    hsq[:, dvt * 512:(dvt + 1) * 512],
                    start=(dvt == 0), stop=(dvt == 1)),
                    inc=(tm if dvt == 1 else None))
            ssd = tm.n
            kb.wait("scalar", tm, ssd)
            sca(lambda e: e.activation(den_r2, ps[3][0:1, 0:512], AF.Ln,
                                       scale=1.0 / DV, bias=1e-6))
            sca(lambda e: e.activation(den_r2, den_r2, AF.Exp, scale=-0.5),
                inc=sm)
            rfd = sm.n
            kb.wait("sync", sm, rfd)
            dma_s(dram["rows_d"][hs:hs + 1, 1024:1536], den_r2)
            kb.wait("sync", sy)
            lrf = dma_s(rfb[:, :], bc_row(dram["rows_d"], hs, 1024, 512))
            kb.wait("vector", sy, lrf)
            kb.wait("vector", sy, ls2)
            for dvt in range(2):
                dcol = 2 * h + dvt
                vec(lambda e, dvt=dvt, dcol=dcol: e.scalar_tensor_tensor(
                    stg32[:, dvt * 512:(dvt + 1) * 512],
                    htmp[:, dvt * 512:(dvt + 1) * 512], mhcol[:, dcol:dcol + 1],
                    rfb[:, :], ALU.mult, ALU.mult))
                vec(lambda e, dvt=dvt, dcol=dcol, sc=sc: e.tensor_mul(
                    bigB[:, dcol * TOK + sc * 512: dcol * TOK + (sc + 1) * 512],
                    stg32[:, dvt * 512:(dvt + 1) * 512],
                    sigt[:, dvt * 512:(dvt + 1) * 512]),
                    inc=(vm if dvt == 1 else None))

    mixer_sc(0)

    def zb_post(op):
        h = op["h"]
        kb.wait("vector", tm)
        kb.wait("vector", sy, bccol_ready)
        vec(lambda e, h=h: e.scalar_tensor_tensor(
            zsb[:, h * 264:h * 264 + 257], zsb[:, h * 264:h * 264 + 257],
            bccol[:, H + h:H + h + 1], ps[h % 4][:, 0:257],
            ALU.mult, ALU.add), inc=vm)

    for wave in range(2):
        heads = list(range(wave * 4, wave * 4 + 4))
        seq = [dict(h=h, slot=h % 2, bias=biasB[:, h:h + 1],
                    bcol4=obcols[:, h * 8:h * 8 + 4], pz=ps[h % 4],
                    first=True, last=True, post=zb_post) for h in heads]
        run_dz_seq(seq)
        vfence()
        kb.wait("tensor", vm)

    if taps:
        kb.wait("sync", vm)
        dma_s(dram["tap_zb"], zsb[:, :])
        dma_s(dram["tap_w4"], w4[:, :])
        dma_s(dram["tap_kwB"], kw[:, :])
        dma_s(dram["tap_bcd"], dram["bc_d"][0:1, 0:24])
    mixer_sc(1)
    hout_ready = vm.n
    if taps:
        kb.wait("sync", vm, hout_ready)
        dma_s(dram["tap_hout"], bigB[:, :])

    kb.wait("tensor", vm, hout_ready)

    def wo_evac(mi, pa, pb, ms):
        kb.wait("vector", tm, ms)
        for half in range(2):
            pp = pa if half == 0 else pb
            c0 = mi * TOK + half * 512
            b0 = half * 512
            vec(lambda e, c0=c0, b0=b0: e.tensor_mul(
                stg32[:, 0:512], xres[:, c0:c0 + 512], sb_s2[:, b0:b0 + 512]))
            vec(lambda e, c0=c0, b0=b0: e.tensor_add(
                stg32[:, 0:512], stg32[:, 0:512], sb_m2[:, b0:b0 + 512]))
            vec(lambda e, c0=c0, pp=pp: e.tensor_add(
                stg32[:, 512:1024], stg32[:, 0:512], ps[pp][:, 0:512]))
            vec(lambda e, c0=c0: e.tensor_copy(bigA[:, c0:c0 + 512],
                                               stg32[:, 512:1024]))
            vec(lambda e, c0=c0: e.tensor_add(
                xres[:, c0:c0 + 512], xres[:, c0:c0 + 512], stg32[:, 512:1024]),
                inc=(vm if half == 1 else None))

    proj("wout_t", KD, (0, 1), wo_evac)

    def ffn_pass(wg_nm, wu_nm, wd_nm):
        kb.wait("tensor", vm)
        for hb in range(4):
            for i in range(16):
                ht = hb * 16 + i
                biG = ht % 2
                biU = 2 + ht % 2
                kb.wait("gpsimd", tm, wlast.get(f"b{biG}", 0))
                lg = dma_g(wb[biG][:, :], dram[wg_nm][ht])
                kb.wait("gpsimd", tm, wlast.get(f"b{biU}", 0))
                lu = dma_g(wb[biU][:, :], dram[wu_nm][ht])
                for half in range(2):
                    kb.wait("tensor", gd, lg)
                    kb.wait("tensor", sm)
                    for a in range(KD):
                        ten(lambda e, half=half, a=a, biG=biG: e.matmul(
                            ps[half][:, 0:512], wb[biG][:, a * 128:(a + 1) * 128],
                            bigA[:, a * TOK + half * 512:
                                 a * TOK + half * 512 + 512],
                            start=(a == 0), stop=(a == KD - 1)),
                            inc=(tm if a == KD - 1 else None))
                    g_ms = tm.n
                    kb.wait("tensor", gd, lu)
                    kb.wait("tensor", vm)
                    for a in range(KD):
                        ten(lambda e, half=half, a=a, biU=biU: e.matmul(
                            ps[2 + half][:, 0:512],
                            wb[biU][:, a * 128:(a + 1) * 128],
                            bigA[:, a * TOK + half * 512:
                                 a * TOK + half * 512 + 512],
                            start=(a == 0), stop=(a == KD - 1)),
                            inc=(tm if a == KD - 1 else None))
                    u_ms = tm.n
                    kb.wait("scalar", tm, g_ms)
                    kb.wait("scalar", vm)
                    sca(lambda e, half=half: e.activation(
                        silu_sb[:, 0:512],
                        ps[half][:, 0:512], AF.Silu), inc=sm)
                    sil = sm.n
                    kb.wait("vector", tm, u_ms)
                    kb.wait("vector", sm, sil)
                    vec(lambda e, i=i, half=half: e.tensor_mul(
                        bigB[:, i * 1024 + half * 512:
                             i * 1024 + half * 512 + 512],
                        silu_sb[:, 0:512],
                        ps[2 + half][:, 0:512]), inc=vm)
                wlast[f"b{biG}"] = tm.n
                wlast[f"b{biU}"] = tm.n
            gated_done = vfence()
            for dt in range(KD):
                biD = 4 + dt % 2
                kb.wait("gpsimd", tm, wlast.get(f"b{biD}", 0))
                ldw = dma_g(wb[biD][:, :],
                            dram[wd_nm][dt][:, hb * 2048:(hb + 1) * 2048])
                kb.wait("tensor", gd, ldw)
                kb.wait("tensor", vm, gated_done)
                for half in range(2):
                    kb.wait("tensor", vm)
                    for k in range(16):
                        ten(lambda e, half=half, k=k, biD=biD: e.matmul(
                            ps[4 + half][:, 0:512],
                            wb[biD][:, k * 128:(k + 1) * 128],
                            bigB[:, k * 1024 + half * 512:
                                 k * 1024 + half * 512 + 512],
                            start=(k == 0), stop=(k == 15)),
                            inc=(tm if k == 15 else None))
                    dd = tm.n
                    kb.wait("vector", tm, dd)
                    c0 = dt * TOK + half * 512
                    vec(lambda e, c0=c0, half=half: e.tensor_add(
                        xres[:, c0:c0 + 512], xres[:, c0:c0 + 512],
                        ps[4 + half][:, 0:512]), inc=vm)
                wlast[f"b{biD}"] = tm.n

    ln_stats(bigA)
    kb.wait("vector", tm)
    for half in range(2):
        vec(lambda e, half=half: e.tensor_scalar(
            rowB[:, half * 512:half * 512 + 512], ps[2 + half][0:1, 0:512],
            1.0 / D, None, ALU.mult), inc=(vm if half == 1 else None))
    kb.wait("scalar", vm)
    sca(lambda e: e.activation(rowC[:, :], rowB[:, :], AF.Ln, bias=1e-6))
    sca(lambda e: e.activation(rowC[:, :], rowC[:, :], AF.Exp, scale=-0.5), inc=sm)
    row_bcasts(1, [(rowC, sb_s2)])
    kb.wait("vector", sy)
    for a in range(KD):
        for half in range(2):
            c0 = a * TOK + half * 512
            vec(lambda e, a=a, c0=c0, half=half: e.scalar_tensor_tensor(
                bigA[:, c0:c0 + 512], bigA[:, c0:c0 + 512], n2col[:, a:a + 1],
                sb_s2[:, half * 512:half * 512 + 512], ALU.mult, ALU.mult),
                inc=(vm if (a == KD - 1 and half == 1) else None))
    ffn_pass("wga_t", "wua_t", "wda_t")
    if taps:
        kb.wait("sync", vm)
        kb.wait("sync", tm)
        dma_s(dram["tap_s1"], xres[:, :])

    def ln_and_norm(phase):
        kb.wait("vector", tm)
        cast_to_bf16(bigA, xres)
        ln_stats(bigA)
        ln_rows(False)
        row_bcasts(phase, [(rowC, sb_s2), (rowA, sb_m2)])
        kb.wait("vector", sy)
        for a in range(KD):
            for half in range(2):
                c0 = a * TOK + half * 512
                b0 = half * 512
                vec(lambda e, c0=c0, b0=b0: e.tensor_mul(
                    stg32[:, 0:512], xres[:, c0:c0 + 512], sb_s2[:, b0:b0 + 512]))
                vec(lambda e, c0=c0, b0=b0: e.tensor_add(
                    stg32[:, 0:512], stg32[:, 0:512], sb_m2[:, b0:b0 + 512]))
                vec(lambda e, c0=c0: e.tensor_copy(bigA[:, c0:c0 + 512],
                                                   stg32[:, 0:512]),
                    inc=(vm if (a == KD - 1 and half == 1) else None))
        return vm.n

    ln_and_norm(2)
    ffn_pass("wgf_t", "wuf_t", "wdf_t")
    if taps:
        kb.wait("sync", vm)
        kb.wait("sync", tm)
        dma_s(dram["tap_s2"], xres[:, :])

    X_ready = ln_and_norm(3)

    kb.wait("tensor", vm, X_ready)
    for a in range(KD):
        for ct in range(8):
            pslot = ((a * 8 + ct) % 2) * 512
            kb.wait("tensor", vm)
            ten(lambda e, a=a, ct=ct, pslot=pslot: e.transpose(
                psT[:, pslot:pslot + 128],
                bigA[:, a * TOK + ct * 128: a * TOK + (ct + 1) * 128],
                identb[:, :]), inc=tm)
            trd = tm.n
            kb.wait("vector", tm, trd)
            vec(lambda e, a=a, ct=ct, pslot=pslot: e.tensor_copy(
                bigB[:, ct * 2048 + a * 128: ct * 2048 + (a + 1) * 128],
                psT[:, pslot:pslot + 128]), inc=vm)
    XT_ready = vm.n

    kb.wait("tensor", vm, XT_ready)
    for mi in range(KD):
        kb.wait("tensor", vm)
        for jn in range(4):
            for ct in range(8):
                ten(lambda e, mi=mi, jn=jn, ct=ct: e.matmul(
                    ps[jn][:, 0:512],
                    bigB[:, ct * 2048 + mi * 128: ct * 2048 + (mi + 1) * 128],
                    bigB[:, ct * 2048 + jn * 512: ct * 2048 + jn * 512 + 512],
                    start=(ct == 0), stop=(ct == 7)),
                    inc=(tm if (jn == 3 and ct == 7) else None))
        ed = tm.n
        kb.wait("vector", tm, ed)
        for seg in range(2):
            kb.wait("vector", sy)
            for jn in range(2):
                vec(lambda e, seg=seg, jn=jn: e.tensor_copy(
                    stg32[:, jn * 512:(jn + 1) * 512],
                    ps[seg * 2 + jn][:, 0:512]), inc=(vm if jn == 1 else None))
            kb.wait("sync", vm)
            dma_s(dram["E_part_d"][mi * 128:(mi + 1) * 128,
                                   seg * 1024:(seg + 1) * 1024], stg32[:, :])
    E_staged = sy.n

    kb.wait("gpsimd", sy, E_staged)
    coll("ReduceScatter", ALU.add, [dram["E_part_d"]], [dram["E_red_d"]], ccs[2])
    kb.raw_wait1("sync", ccs[2].h)

    for pi in range(4):
        kb.wait("sync", vm)
        l0 = dma_s(stg32[:, :], dram["E_red_d"][pi * 128:(pi + 1) * 128, 0:1024])

        def maxtree(dst):
            w = 512
            while w >= 1:
                vec(lambda e, w=w: e.tensor_max(stg32[:, 0:w], stg32[:, 0:w],
                                                stg32[:, w:2 * w]))
                w //= 2
            vec(lambda e, dst=dst: e.tensor_copy(dst, stg32[:, 0:1]))
            vfence(4)
        kb.wait("vector", sy, l0)
        maxtree(negmax[:, :])
        kb.wait("sync", vm)
        l1 = dma_s(stg32[:, :], dram["E_red_d"][pi * 128:(pi + 1) * 128,
                                                1024:2048])
        kb.wait("vector", sy, l1)
        maxtree(esum_a[:, :])
        kb.wait("sync", vm)
        dma_s(dram["sm_d"][0:1, 0:128].rearrange("r c -> c r"), negmax[:, :])
        dma_s(dram["sm_d"][0:1, 128:256].rearrange("r c -> c r"), esum_a[:, :])
        smx_out = sy.n
        kb.wait("sync", sy, smx_out)
        dma_s(negmax[:, :], dram["sm_d"][0:1, 0:128].rearrange("r c -> c r"))
        dma_s(esum_a[:, :], dram["sm_d"][0:1, 128:256].rearrange("r c -> c r"))
        kb.wait("vector", sy)
        vec(lambda e: e.tensor_max(negmax[:, :], negmax[:, :], esum_a[:, :]))
        vec(lambda e: e.tensor_scalar(negmax[:, :], negmax[:, :], -1.0, None,
                                      ALU.mult))
        nm_ms = vfence(4)
        for hf in range(2):
            kb.wait("sync", tm)
            kb.wait("sync", sm)
            lh = dma_s(stg32[:, :], dram["E_red_d"][pi * 128:(pi + 1) * 128,
                                                    hf * 1024:(hf + 1) * 1024])
            kb.wait("scalar", sy, lh)
            kb.wait("scalar", vm, nm_ms)
            sca(lambda e, hf=hf: e.activation(
                stage[:, :], stg32[:, :], AF.Exp, bias=negmax[:, 0:1],
                accum_out=(esum_a if hf == 0 else esum_b)[:, 0:1]), inc=sm)
            ex_ms = sm.n
            kb.wait("tensor", sm, ex_ms)
            for jt in range(8):
                pslot = (jt % 2) * 512
                kb.wait("tensor", vm)
                ten(lambda e, jt=jt, pslot=pslot: e.transpose(
                    psT[:, pslot:pslot + 128], stage[:, jt * 128:(jt + 1) * 128],
                    identb[:, :]), inc=tm)
                kb.wait("vector", tm)
                vec(lambda e, jt=jt, pslot=pslot: e.tensor_copy(
                    ktb[:, jt * 128:(jt + 1) * 128], psT[:, pslot:pslot + 128]),
                    inc=vm)
            vfence()
            kb.wait("sync", vm)
            dma_s(dram["attnT_in_d"][hf * 1024:(hf + 1) * 1024,
                                     pi * 128:(pi + 1) * 128]
                  .rearrange("(jt p) i -> p jt i", p=128),
                  ktb[:, 0:1024].rearrange("p (jt i) -> p jt i", i=128))
        kb.wait("sync", sm)
        dma_s(dram["sm_d"][0:1, 0:128].rearrange("r c -> c r"), esum_a[:, :])
        dma_s(dram["sm_d"][0:1, 128:256].rearrange("r c -> c r"), esum_b[:, :])
        es_out = sy.n
        kb.wait("sync", sy, es_out)
        dma_s(esum_a[:, :], dram["sm_d"][0:1, 0:128].rearrange("r c -> c r"))
        dma_s(esum_b[:, :], dram["sm_d"][0:1, 128:256].rearrange("r c -> c r"))
        kb.wait("vector", sy)
        vec(lambda e: e.tensor_add(esum_a[:, :], esum_a[:, :], esum_b[:, :]))
        vfence()
        vec(lambda e, pi=pi: e.reciprocal(erec4[:, pi:pi + 1], esum_a[:, :]),
            inc=vm)
    vfence(4)
    kb.wait("sync", vm)
    for pi in range(4):
        dma_s(dram["recip_in_d"][0:1, pi * 128:(pi + 1) * 128]
              .rearrange("r c -> c r"), erec4[:, pi:pi + 1])
    attnT_staged = sy.n

    kb.wait("gpsimd", sy, attnT_staged)
    coll("AllGather", ALU.bypass, [dram["attnT_in_d"]], [dram["attnT_out_d"]],
         ccs[3])
    coll("AllGather", ALU.bypass, [dram["recip_in_d"]], [dram["recip_out_d"]],
         ccs[4])
    kb.raw_wait1("sync", ccs[3].h)
    kb.raw_wait1("sync", ccs[4].h)
    if taps:
        dma_s(dram["tap_ered"], dram["E_red_d"][:, :])
        dma_s(dram["tap_at"], dram["attnT_in_d"][:, :])
        dma_s(dram["tap_recip"], dram["recip_out_d"][:, :])

    for mi in range(KD):
        r = mi // 4
        c0b = (mi % 4) * 128
        kb.wait("sync", tm)
        kb.wait("sync", vm)
        lat = dma_s(ktb[:, 0:2048].rearrange("p (a i) -> p a i", i=128),
                    dram["attnT_out_d"][r * 2048:(r + 1) * 2048, c0b:c0b + 128]
                    .rearrange("(a p) i -> p a i", p=128))
        lrg = dma_s(rgcol[:, :],
                    dram["recip_out_d"][mi // 4:mi // 4 + 1,
                                        (mi % 4) * 128:(mi % 4) * 128 + 128]
                    .rearrange("r c -> c r"))
        kb.wait("vector", sy, lrg)
        vec(lambda e: e.tensor_mul(rgcol[:, :], rgcol[:, :], gcol[:, :]), inc=vm)
        rg_ms = vm.n
        kb.wait("tensor", sy, lat)
        kb.wait("tensor", vm)
        for half in range(2):
            for a in range(KD):
                ten(lambda e, half=half, a=a: e.matmul(
                    ps[4 + half][:, 0:512], ktb[:, a * 128:(a + 1) * 128],
                    bigA[:, a * TOK + half * 512: a * TOK + half * 512 + 512],
                    start=(a == 0), stop=(a == KD - 1)),
                    inc=(tm if a == KD - 1 else None))
        ap_ms = tm.n
        kb.wait("vector", tm, ap_ms)
        kb.wait("vector", vm, rg_ms)
        for half in range(2):
            c0 = mi * TOK + half * 512
            b0 = half * 512
            pp = 4 + half
            vec(lambda e, c0=c0, b0=b0: e.tensor_mul(
                stg32[:, 0:512], xres[:, c0:c0 + 512], sb_s2[:, b0:b0 + 512]))
            vec(lambda e, c0=c0, b0=b0: e.tensor_add(
                stg32[:, 0:512], stg32[:, 0:512], sb_m2[:, b0:b0 + 512]))
            vec(lambda e, c0=c0, pp=pp: e.scalar_tensor_tensor(
                stg32[:, 512:1024], ps[pp][:, 0:512], rgcol[:, 0:1],
                stg32[:, 0:512], ALU.mult, ALU.add))
            vec(lambda e, c0=c0: e.tensor_add(
                xres[:, c0:c0 + 512], xres[:, c0:c0 + 512], stg32[:, 512:1024]),
                inc=(vm if half == 1 else None))
    if taps:
        kb.wait("sync", vm)
        dma_s(dram["tap_s3"], xres[:, :])

    ln_and_norm(4)
    ffn_pass("wgf_t", "wuf_t", "wdf_t")

    kb.wait("sync", vm)
    kb.wait("sync", tm)
    dma_s(dram["xout"], xres[:, :])
    kb.wait("sync", sy)

    with nc.Block() as block:
        block.tensor(lambda e: [f(e) for f in kb.streams["tensor"]])
        block.vector(lambda e: [f(e) for f in kb.streams["vector"]])
        block.scalar(lambda e: [f(e) for f in kb.streams["scalar"]])
        block.gpsimd(lambda e: [f(e) for f in kb.streams["gpsimd"]])
        block.sync(lambda e: [f(e) for f in kb.streams["sync"]])
    return nc


def host_prep(inputs):
    f32 = np.float32
    g = lambda n: np.asarray(inputs[n], f32)
    wif = np.concatenate([g("L0_attn_a_w_ig"), g("L0_attn_a_w_fg")], axis=1)
    shared = {
        "wq_t": tile_lhsT(g("L0_attn_a_wq")),
        "wk_t": tile_lhsT(g("L0_attn_a_wk")),
        "wv_t": tile_lhsT(g("L0_attn_a_wv")),
        "wif_b": np.ascontiguousarray(
            wif.reshape(KD, 128, 16).transpose(1, 0, 2).reshape(128, KD * 16)),
        "wog_t": tile_lhsT(g("L0_attn_a_w_og")),
        "wout_t": tile_lhsT(g("L0_attn_a_w_out")),
        "wga_t": tile_lhsT(g("L0_attn_a_w_gate")),
        "wua_t": tile_lhsT(g("L0_attn_a_w_up")),
        "wda_t": tile_lhsT(g("L0_attn_a_w_down")),
        "wgf_t": tile_lhsT(np.ascontiguousarray(g("L0_ffn_w_gate").T)),
        "wuf_t": tile_lhsT(np.ascontiguousarray(g("L0_ffn_w_up").T)),
        "wdf_t": tile_lhsT(np.ascontiguousarray(g("L0_ffn_w_down").T)),
        "n1col": np.ascontiguousarray(g("L0_attn_a_norm1_w").reshape(KD, 128).T),
        "n2col": np.ascontiguousarray(g("L0_attn_a_norm2_w").reshape(KD, 128).T),
        "mhcol": np.ascontiguousarray(g("L0_attn_a_mh_w").reshape(KD, 128).T),
        "bif": np.concatenate([g("L0_attn_a_b_ig"),
                               g("L0_attn_a_b_fg")]).reshape(16, 1).astype(f32),
        "gamma": np.ascontiguousarray(
            np.broadcast_to(g("L0_attn_b_gamma").reshape(1, 1), (128, 1))),
        "tri": np.where(np.triu(np.ones((128, 128))) > 0, 0.0,
                        NEG).astype(np.float32),
        "identb": np.eye(128, dtype=BF),
        "identf": np.eye(32, dtype=np.float32),
        "onesb": np.ones((128, 1), dtype=BF),
    }
    x = g("L0_attn_a_x")
    in_maps = []
    for c in range(8):
        b, j = c // 4, c % 4
        xc = x[b, j * TOK:(j + 1) * TOK]
        x_pre = np.ascontiguousarray(
            xc.T.reshape(KD, 128, TOK).transpose(1, 0, 2).reshape(128, KD * TOK))
        addmask = np.where(np.arange(8) < 2 * j, 0.0, NEG).astype(f32)
        onehotF = np.zeros((8, 4), f32)
        if j > 0:
            onehotF[:, j - 1] = 1.0
        m = dict(shared)
        m["x_pre"] = x_pre
        m["addmask"] = np.ascontiguousarray(
            np.broadcast_to(addmask, (128, 8)).astype(f32))
        m["onehotF"] = onehotF
        m["initmask"] = np.full((8, 1), 0.0 if j > 0 else NEG, f32)
        in_maps.append(m)
    return in_maps


def assemble(results):
    out = np.empty((2, SEQ, D), np.float32)
    for c in range(8):
        b, j = c // 4, c % 4
        o = results[c]["xout"]
        out[b, j * TOK:(j + 1) * TOK] = \
            o.reshape(128, KD, TOK).transpose(1, 0, 2).reshape(D, TOK).T
    return out


_nc_cache = {}


def kernel(**inputs):
    in_maps = host_prep(inputs)
    if "nc" not in _nc_cache:
        _nc_cache["nc"] = build_nc(taps=False)
    res = run_bass_kernel_spmd(_nc_cache["nc"], in_maps, core_ids=list(range(8)))
    return assemble(res.results)
